# version 03
# baseline (speedup 1.0000x reference)
import contextlib
import numpy as np

import concourse.bass as bass
import concourse.mybir as mybir
from concourse.bass_utils import run_bass_kernel_spmd
from concourse.aot_env import allow_ici_all_to_all

F32 = mybir.dt.float32
BF16 = mybir.dt.bfloat16
AF = mybir.ActivationFunctionType
ALU = mybir.AluOpType
AX = mybir.AxisListType

NCORE = 8
S = 16384
H = 2048
TC = 512
LN_EPS = 1e-5



class Prog:

    def __init__(self, nc, sem):
        self.nc = nc
        self.sem = sem
        self.steps = []

    def step(self, engine, fn, dma=False, nowait=False, noinc=False):
        self.steps.append((engine, fn, 0 if noinc else (16 if dma else 1),
                           nowait))

    def dma(self, q, out, in_):
        self.step(q, lambda nc: getattr(nc, q).dma_start(out=out, in_=in_),
                  dma=True)

    def mm(self, out, lhsT, rhs, start, stop, sync=False):
        self.step("tensor",
                  lambda nc: nc.tensor.matmul(out, lhsT, rhs,
                                              start=start, stop=stop),
                  nowait=(not start) and (not sync),
                  noinc=(not stop) and (not sync))

    def transpose(self, out, in_, ident):
        self.step("tensor", lambda nc: nc.tensor.transpose(out, in_, ident))

    def act(self, out, in_, func, bias=0.0, scale=1.0):
        self.step("scalar",
                  lambda nc: nc.scalar.activation(out, in_, func,
                                                  bias=bias, scale=scale))

    def smul(self, out, in_, c):
        self.step("scalar", lambda nc: nc.scalar.mul(out, in_, c))

    def copy(self, out, in_):
        self.step("scalar", lambda nc: nc.scalar.copy(out, in_))

    def vcopy(self, out, in_):
        self.step("vector", lambda nc: nc.vector.tensor_copy(out, in_))

    def tt(self, out, a, b, op):
        self.step("vector", lambda nc: nc.vector.tensor_tensor(out, a, b, op))

    def add(self, out, a, b):
        self.tt(out, a, b, ALU.add)

    def sub(self, out, a, b):
        self.tt(out, a, b, ALU.subtract)

    def mul(self, out, a, b):
        self.tt(out, a, b, ALU.mult)

    def vmax(self, out, a, b):
        self.tt(out, a, b, ALU.max)

    def ts(self, out, in0, s1, s2=None, op0=ALU.mult, op1=ALU.mult):
        if s2 is None:
            self.step("vector", lambda nc: nc.vector.tensor_scalar(
                out, in0, s1, None, op0))
        else:
            self.step("vector", lambda nc: nc.vector.tensor_scalar(
                out, in0, s1, s2, op0, op1))

    def stt(self, out, in0, scalar, in1, op0, op1):
        self.step("vector", lambda nc: nc.vector.scalar_tensor_tensor(
            out, in0, scalar, in1, op0, op1))

    def scan(self, out, a, b, initial):
        self.step("vector", lambda nc: nc.vector.tensor_tensor_scan(
            out, a, b, initial, ALU.mult, ALU.add))

    def recip(self, out, in_):
        self.step("vector", lambda nc: nc.vector.reciprocal(out, in_))

    def rmax(self, out, in_):
        self.step("vector", lambda nc: nc.vector.reduce_max(out, in_, AX.X))

    def rsum(self, out, in_):
        self.step("vector", lambda nc: nc.vector.reduce_sum(out, in_, AX.X))

    def memset(self, ap, v):
        self.step("vector", lambda nc: nc.vector.memset(ap, v))

    def build(self, block):
        nc = self.nc
        sem = self.sem
        steps = self.steps
        engines = ["sync", "scalar", "vector", "tensor", "gpsimd"]

        def make_body(name):
            def body(eng):
                cum = 0
                for item in steps:
                    (en, fn, inc, nowait) = item
                    if en == name:
                        if cum > 0 and not nowait:
                            eng.wait_ge(sem, cum)
                        inst = fn(nc)
                        if inc > 0:
                            inst.then_inc(sem, inc)
                    cum += inc
                eng.wait_ge(sem, cum)
            return body

        for name in engines:
            getattr(block, name)(make_body(name))


def _softplus(p, out, in_, t1, t2):
    p.act(t1, in_, AF.Abs)
    p.act(t1, t1, AF.Exp, scale=-1.0)
    p.act(t1, t1, AF.Ln, bias=1.0)
    p.act(t2, in_, AF.Relu)
    p.add(out, t2, t1)


def _load_wslab(p, scr, dst_b, w_ap, scr_cols=2048):
    ktot, m = w_ap.shape[0], w_ap.shape[1]
    if ktot == 128:
        o = 0
        while o < m:
            e2 = min(m, o + scr_cols)
            p.dma("sync", scr[:, 0:e2 - o], w_ap[:, o:e2])
            p.vcopy(dst_b[:, o:e2], scr[:, 0:e2 - o])
            o = e2
        return
    kt = ktot // 128
    src3 = w_ap.rearrange("(k p) m -> p k m", p=128)
    kp = max(1, min(scr_cols // m, 8))
    k0 = 0
    while k0 < kt:
        k1 = min(kt, k0 + kp)
        cols = (k1 - k0) * m
        p.dma("sync",
              scr[:, 0:cols].rearrange("p (k m) -> p k m", k=k1 - k0),
              src3[:, k0:k1, :])
        p.vcopy(dst_b[:, k0 * m:k1 * m], scr[:, 0:cols])
        k0 = k1


def _load_vec(p, dst, v_h):
    kt = v_h.shape[0] // 128
    for k in range(kt):
        p.dma("sync", dst[:, k:k + 1], v_h[k * 128:(k + 1) * 128, :])


def _ln_ct(p, r_tiles, out_tiles, ones_col, ones_row, ps_s1, ps_s2, ps_bc,
           sq_scr, lnr, rstd_bc, mr_bc, n_ch, tc):
    ng = len(r_tiles)
    rm = lnr[:, 0 * tc:1 * tc]
    rv = lnr[:, 1 * tc:2 * tc]
    rs = lnr[:, 2 * tc:3 * tc]
    rr2 = lnr[:, 3 * tc:4 * tc]
    rmr = lnr[:, 4 * tc:5 * tc]
    for g in range(ng):
        p.mm(ps_s1[0:1, :tc], ones_col[:, 0:1], r_tiles[g],
             start=(g == 0), stop=(g == ng - 1))
    for g in range(ng):
        p.act(sq_scr[:, :tc], r_tiles[g], AF.Square)
        p.mm(ps_s2[0:1, :tc], ones_col[:, 0:1], sq_scr[:, :tc],
             start=(g == 0), stop=(g == ng - 1), sync=True)
    inv = 1.0 / n_ch
    p.smul(rm, ps_s1[0:1, :tc], inv)
    p.smul(rv, ps_s2[0:1, :tc], inv)
    p.act(rs, rm, AF.Square)
    p.sub(rv, rv, rs)
    p.ts(rv, rv, LN_EPS, op0=ALU.add)
    p.act(rs, rv, AF.Sqrt)
    p.recip(rr2, rs)
    p.mul(rmr, rm, rr2)
    p.mm(ps_bc[:, :tc], ones_row[0:1, :], rr2, start=True, stop=True)
    p.copy(rstd_bc[:, :tc], ps_bc[:, :tc])
    p.mm(ps_bc[:, :tc], ones_row[0:1, :], rmr, start=True, stop=True)
    p.copy(mr_bc[:, :tc], ps_bc[:, :tc])
    for g in range(ng):
        p.mul(out_tiles[g], r_tiles[g], rstd_bc[:, :tc])
        p.sub(out_tiles[g], out_tiles[g], mr_bc[:, :tc])



def build_A():
    nc = bass.Bass("TRN2", target_bir_lowering=False)
    RC = 256
    M2 = 2 * RC
    xT = nc.dram_tensor("xT", [H, S], F32, kind="ExternalInput")
    wxyT = nc.dram_tensor("wxyT", [H, M2], F32, kind="ExternalInput")
    igw = nc.dram_tensor("igw", [RC, 128], F32, kind="ExternalInput")
    igb = nc.dram_tensor("igb", [RC, 1], F32, kind="ExternalInput")
    agw = nc.dram_tensor("agw", [RC, 128], F32, kind="ExternalInput")
    agb = nc.dram_tensor("agb", [RC, 1], F32, kind="ExternalInput")
    apar = nc.dram_tensor("apar", [RC, 1], F32, kind="ExternalInput")
    cw = nc.dram_tensor("cw", [RC, 4], F32, kind="ExternalInput")
    cb = nc.dram_tensor("cb", [RC, 1], F32, kind="ExternalInput")
    z_h = nc.dram_tensor("z", [RC, S], F32, kind="ExternalOutput")

    NT = S // TC
    with contextlib.ExitStack() as st:
        e = st.enter_context
        wscr = e(nc.sbuf_tensor("wscr", [128, 2048], F32))
        wxy_b = e(nc.sbuf_tensor("wxy_b", [128, 16 * M2], BF16))
        igw_b = e(nc.sbuf_tensor("igw_b", [128, 2 * 128], BF16))
        agw_b = e(nc.sbuf_tensor("agw_b", [128, 2 * 128], BF16))
        igb_s = e(nc.sbuf_tensor("igb_s", [128, 2], F32))
        agb_s = e(nc.sbuf_tensor("agb_s", [128, 2], F32))
        apar_s = e(nc.sbuf_tensor("apar_s", [128, 2], F32))
        cw_s = e(nc.sbuf_tensor("cw_s", [128, 8], F32))
        cb_s = e(nc.sbuf_tensor("cb_s", [128, 2], F32))
        sp_s = e(nc.sbuf_tensor("sp_s", [128, 2], F32))
        t1s = e(nc.sbuf_tensor("t1s", [128, 1], F32))
        t2s = e(nc.sbuf_tensor("t2s", [128, 1], F32))
        xf = e(nc.sbuf_tensor("xf", [128, 16 * TC], F32))
        xb = e(nc.sbuf_tensor("xb", [128, 16 * TC], BF16))
        win = e(nc.sbuf_tensor("win", [128, 2 * (TC + 3)], F32))
        xc = e(nc.sbuf_tensor("xc", [128, 2 * TC], F32))
        xcb = e(nc.sbuf_tensor("xcb", [128, 2 * TC], BF16))
        gy = e(nc.sbuf_tensor("gy", [128, 2 * TC], F32))
        g1 = e(nc.sbuf_tensor("g1", [128, TC], F32))
        g2 = e(nc.sbuf_tensor("g2", [128, TC], F32))
        s1 = e(nc.sbuf_tensor("s1", [128, TC], F32))
        s3 = e(nc.sbuf_tensor("s3", [128, TC], F32))
        av = e(nc.sbuf_tensor("av", [128, TC], F32))
        nx = e(nc.sbuf_tensor("nx", [128, TC], F32))
        hb = e(nc.sbuf_tensor("hb", [128, 2 * TC], F32))
        zt = e(nc.sbuf_tensor("zt", [128, TC], F32))
        carry = e(nc.sbuf_tensor("carry", [128, 2], F32))
        ps = e(nc.psum_tensor("ps", [128, TC], F32))
        sem = e(nc.semaphore("G"))
        block = e(nc.Block())

        p = Prog(nc, sem)
        _load_wslab(p, wscr, wxy_b, wxyT[:, :])
        _load_wslab(p, wscr, igw_b, igw[:, :])
        _load_wslab(p, wscr, agw_b, agw[:, :])
        for (v_h, dst) in ((igb, igb_s), (agb, agb_s), (apar, apar_s),
                           (cb, cb_s)):
            _load_vec(p, dst, v_h)
        p.dma("sync", cw_s[:, :].rearrange("p (k o) -> p k o", k=2),
              cw[:, :].rearrange("(k p) o -> p k o", p=128))
        for g in range(2):
            _softplus(p, sp_s[:, g:g + 1], apar_s[:, g:g + 1], t1s[:, 0:1], t2s[:, 0:1])

        for it in range(NT):
            t0 = it * TC
            xr = xT[:, t0:t0 + TC].rearrange("(g p) t -> p g t", p=128)
            for i in range(4):
                p.dma("sync",
                      xf[:, i * 4 * TC:(i + 1) * 4 * TC].rearrange(
                          "p (g t) -> p g t", g=4),
                      xr[:, i * 4:(i + 1) * 4, :])
            for g in range(16):
                p.copy(xb[:, g * TC:(g + 1) * TC], xf[:, g * TC:(g + 1) * TC])
            for m in range(4):
                for k in range(16):
                    p.mm(ps[:, :],
                         wxy_b[:, k * M2 + m * 128:k * M2 + (m + 1) * 128],
                         xb[:, k * TC:(k + 1) * TC],
                         start=(k == 0), stop=(k == 15))
                if m < 2:
                    w0 = m * (TC + 3)
                    p.copy(win[:, w0 + 3:w0 + TC + 3], ps[:, :])
                else:
                    g = m - 2
                    p.act(gy[:, g * TC:(g + 1) * TC], ps[:, :], AF.Gelu)
            for g in range(2):
                w0 = g * (TC + 3)
                if it == 0:
                    p.memset(win[:, w0:w0 + 3], 0.0)
                xcg = xc[:, g * TC:(g + 1) * TC]
                p.ts(xcg, win[:, w0:w0 + TC], cw_s[:, 4 * g:4 * g + 1])
                for i in range(1, 4):
                    p.stt(xcg, win[:, w0 + i:w0 + i + TC],
                          cw_s[:, 4 * g + i:4 * g + i + 1], xcg,
                          ALU.mult, ALU.add)
                p.ts(xcg, xcg, cb_s[:, g:g + 1], op0=ALU.add)
                p.vcopy(win[:, w0:w0 + 3], win[:, w0 + TC:w0 + TC + 3])
                p.copy(xcb[:, g * TC:(g + 1) * TC], xcg)
                xcbg = xcb[:, g * TC:(g + 1) * TC]
                p.mm(ps[:, :], igw_b[:, g * 128:(g + 1) * 128], xcbg,
                     start=True, stop=True)
                p.ts(g1[:, :], ps[:, :], igb_s[:, g:g + 1], op0=ALU.add)
                p.mm(ps[:, :], agw_b[:, g * 128:(g + 1) * 128], xcbg,
                     start=True, stop=True)
                p.ts(g2[:, :], ps[:, :], agb_s[:, g:g + 1], op0=ALU.add)
                p.act(s1[:, :], g1[:, :], AF.Sigmoid)
                p.act(g2[:, :], g2[:, :], AF.Sigmoid)
                p.ts(g2[:, :], g2[:, :], sp_s[:, g:g + 1], -8.0)
                p.act(av[:, :], g2[:, :], AF.Exp)
                p.act(s3[:, :], av[:, :], AF.Square)
                p.act(s3[:, :], s3[:, :], AF.Sqrt, bias=1.0, scale=-1.0)
                p.mul(nx[:, :], xcg, s1[:, :])
                p.mul(nx[:, :], nx[:, :], s3[:, :])
                hg = hb[:, g * TC:(g + 1) * TC]
                if it == 0:
                    p.scan(hg, av[:, :], nx[:, :], 0.0)
                else:
                    p.scan(hg, av[:, :], nx[:, :], carry[:, g:g + 1])
                p.vcopy(carry[:, g:g + 1], hg[:, TC - 1:TC])
                p.mul(zt[:, :], gy[:, g * TC:(g + 1) * TC], hg)
                p.dma("gpsimd", z_h[g * 128:(g + 1) * 128, t0:t0 + TC],
                      zt[:, :])
        p.build(block)
    return nc



B_PHASES = 4


def build_B():
    nc = bass.Bass("TRN2", target_bir_lowering=False)
    PH = B_PHASES
    E, I = 8, 1408
    NI = I // 128
    zc = nc.dram_tensor("zc", [H, TC], F32, kind="ExternalInput")
    xcd = nc.dram_tensor("xcd", [H, TC], F32, kind="ExternalInput")
    wresT = nc.dram_tensor("wresT", [H, H], F32, kind="ExternalInput")
    gwT = nc.dram_tensor("gwT", [H, E], F32, kind="ExternalInput")
    w1T = nc.dram_tensor("w1T", [E, H, I], F32, kind="ExternalInput")
    w2T = nc.dram_tensor("w2T", [E, H, I], F32, kind="ExternalInput")
    w3T = nc.dram_tensor("w3T", [E, I, H], F32, kind="ExternalInput")
    sw1T = nc.dram_tensor("sw1T", [H, 2 * I], F32, kind="ExternalInput")
    sw2T = nc.dram_tensor("sw2T", [H, 2 * I], F32, kind="ExternalInput")
    sw3T = nc.dram_tensor("sw3T", [2 * I, H], F32, kind="ExternalInput")
    eye_h = nc.dram_tensor("eye", [128, 128], F32, kind="ExternalInput")
    sel_h = nc.dram_tensor("sel8", [8, 8 * 128], F32, kind="ExternalInput")
    x2_h = nc.dram_tensor("x2", [H, TC], F32, kind="ExternalOutput")
    x1s = nc.dram_tensor("x1s", [H, TC], F32)

    with contextlib.ExitStack() as st:
        e = st.enter_context
        zb = e(nc.sbuf_tensor("zb", [128, 16 * TC], BF16))
        scr = e(nc.sbuf_tensor("scr", [128, TC], F32))
        rt = e(nc.sbuf_tensor("rt", [128, 16 * TC], F32))
        x1b = e(nc.sbuf_tensor("x1b", [128, 16 * TC], BF16))
        yac = e(nc.sbuf_tensor("yac", [128, 16 * TC], F32))
        ht = e(nc.sbuf_tensor("ht", [128, NI * TC], BF16))
        h1b = e(nc.sbuf_tensor("h1b", [128, TC], BF16))
        h2b = e(nc.sbuf_tensor("h2b", [128, TC], BF16))
        wscr = e(nc.sbuf_tensor("wscr", [128, 2048], F32))
        wslab = e(nc.sbuf_tensor("wslab", [128, 2048], BF16))
        w3slab = e(nc.sbuf_tensor("w3slab", [128, NI * 128], BF16))
        gw_b = e(nc.sbuf_tensor("gw_b", [128, 16 * E], BF16))
        eyes = e(nc.sbuf_tensor("eyes", [128, 128], F32))
        sels = e(nc.sbuf_tensor("sels", [8, 8 * 128], F32))
        lt = e(nc.sbuf_tensor("lt", [128, 32], F32))
        eqt = e(nc.sbuf_tensor("eqt", [128, 8], F32))
        ext = e(nc.sbuf_tensor("ext", [128, 8], F32))
        rcol = e(nc.sbuf_tensor("rcol", [128, 8], F32))
        comb8 = e(nc.sbuf_tensor("comb8", [8, TC], F32))
        comb_bc = e(nc.sbuf_tensor("comb_bc", [128, TC], BF16))
        sq_scr = e(nc.sbuf_tensor("sq_scr", [128, TC], F32))
        ones_col = e(nc.sbuf_tensor("ones_col", [128, 1], F32))
        ones_row = e(nc.sbuf_tensor("ones_row", [1, 128], F32))
        rstd_bc = e(nc.sbuf_tensor("rstd_bc", [128, TC], F32))
        mr_bc = e(nc.sbuf_tensor("mr_bc", [128, TC], F32))
        lnr = e(nc.sbuf_tensor("lnr", [1, 5 * TC], F32))
        ps_a = e(nc.psum_tensor("ps_a", [128, TC], F32))
        ps_s1 = e(nc.psum_tensor("ps_s1", [128, TC], F32))
        ps_s2 = e(nc.psum_tensor("ps_s2", [128, TC], F32))
        ps_bc = e(nc.psum_tensor("ps_bc", [128, TC], F32))
        sem = e(nc.semaphore("G"))
        block = e(nc.Block())

        p = Prog(nc, sem)
        p.memset(ones_col[:, :], 1.0)
        p.memset(ones_row[:, :], 1.0)
        p.dma("sync", eyes[:, :], eye_h[:, :])
        p.dma("sync", sels[:, :], sel_h[:, :])

        for g in range(16):
            p.dma("sync", scr[:, :], zc[g * 128:(g + 1) * 128, :])
            p.vcopy(zb[:, g * TC:(g + 1) * TC], scr[:, :])

        for m in range(16):
            _load_wslab(p, wscr, wslab, wresT[:, m * 128:(m + 1) * 128])
            for k in range(16):
                p.mm(ps_a[:, :], wslab[:, k * 128:(k + 1) * 128],
                     zb[:, k * TC:(k + 1) * TC],
                     start=(k == 0), stop=(k == 15))
            p.dma("sync", scr[:, :], xcd[m * 128:(m + 1) * 128, :])
            p.add(rt[:, m * TC:(m + 1) * TC], ps_a[:, :], scr[:, :])
        r_tiles = [rt[:, g * TC:(g + 1) * TC] for g in range(16)]
        _ln_ct(p, r_tiles, r_tiles, ones_col, ones_row, ps_s1, ps_s2, ps_bc,
               sq_scr, lnr, rstd_bc, mr_bc, H, TC)
        for g in range(16):
            p.copy(x1b[:, g * TC:(g + 1) * TC], r_tiles[g])
            p.dma("gpsimd", x1s[g * 128:(g + 1) * 128, :], r_tiles[g])

        if PH < 2:
            for g in range(16):
                p.dma("gpsimd", x2_h[g * 128:(g + 1) * 128, :], r_tiles[g])
            p.build(block)
            return nc
        _load_wslab(p, wscr, gw_b, gwT[:, :])
        for s in range(4):
            for k in range(16):
                p.mm(ps_a[:, s * 8:(s + 1) * 8],
                     x1b[:, k * TC + s * 128:k * TC + (s + 1) * 128],
                     gw_b[:, k * E:(k + 1) * E],
                     start=(k == 0), stop=(k == 15))
        p.vcopy(lt[:, :], ps_a[:, 0:32])
        for s in range(4):
            lts = lt[:, s * 8:(s + 1) * 8]
            m1 = rcol[:, 0:1]
            nm1 = rcol[:, 1:2]
            m2 = rcol[:, 2:3]
            sm = rcol[:, 3:4]
            rs = rcol[:, 4:5]
            p.rmax(m1, lts)
            p.ts(nm1, m1, -1.0)
            p.ts(eqt[:, :], lts, m1, op0=ALU.is_equal)
            p.stt(eqt[:, :], eqt[:, :], -1e9, lts, ALU.mult, ALU.add)
            p.rmax(m2, eqt[:, :])
            p.act(ext[:, :], lts, AF.Exp, bias=nm1)
            p.rsum(sm, ext[:, :])
            p.recip(rs, sm)
            p.ts(eqt[:, :], lts, m2, op0=ALU.is_ge)
            p.mul(ext[:, :], ext[:, :], eqt[:, :])
            p.ts(ext[:, :], ext[:, :], rs)
            p.transpose(ps_bc[0:8, 0:128], ext[:, :], eyes[:, :])
            p.vcopy(comb8[:, s * 128:(s + 1) * 128], ps_bc[0:8, 0:128])

        if PH < 3:
            for g in range(16):
                p.dma("gpsimd", x2_h[g * 128:(g + 1) * 128, :], r_tiles[g])
            p.build(block)
            return nc
        ves = [("sh", 0), ("sh", 1)] + [("e", i) for i in range(E)]
        if PH == 3:
            ves = ves[:3]
        for vi, (kind, idx) in enumerate(ves):
            if kind == "e":
                p.mm(ps_bc[:, :], sels[:, idx * 128:(idx + 1) * 128],
                     comb8[:, :], start=True, stop=True)
                p.copy(comb_bc[:, :], ps_bc[:, :])
            for ig in range(NI):
                if kind == "sh":
                    c0 = idx * I + ig * 128
                    a1 = sw1T[:, c0:c0 + 128]
                    a2 = sw2T[:, c0:c0 + 128]
                else:
                    c0 = ig * 128
                    a1 = w1T[idx, :, c0:c0 + 128]
                    a2 = w2T[idx, :, c0:c0 + 128]
                _load_wslab(p, wscr, wslab, a1)
                for k in range(16):
                    p.mm(ps_a[:, :], wslab[:, k * 128:(k + 1) * 128],
                         x1b[:, k * TC:(k + 1) * TC],
                         start=(k == 0), stop=(k == 15))
                p.act(h1b[:, :], ps_a[:, :], AF.Silu)
                _load_wslab(p, wscr, wslab, a2)
                for k in range(16):
                    p.mm(ps_a[:, :], wslab[:, k * 128:(k + 1) * 128],
                         x1b[:, k * TC:(k + 1) * TC],
                         start=(k == 0), stop=(k == 15))
                p.copy(h2b[:, :], ps_a[:, :])
                htg = ht[:, ig * TC:(ig + 1) * TC]
                p.mul(htg, h1b[:, :], h2b[:, :])
                if kind == "e":
                    p.mul(htg, htg, comb_bc[:, :])
            for m in range(16):
                if kind == "sh":
                    a3 = sw3T[idx * I:(idx + 1) * I, m * 128:(m + 1) * 128]
                else:
                    a3 = w3T[idx, :, m * 128:(m + 1) * 128]
                _load_wslab(p, wscr, w3slab, a3)
                for ig in range(NI):
                    p.mm(ps_a[:, :], w3slab[:, ig * 128:(ig + 1) * 128],
                         ht[:, ig * TC:(ig + 1) * TC],
                         start=(ig == 0), stop=(ig == NI - 1))
                if vi == 0:
                    p.copy(yac[:, m * TC:(m + 1) * TC], ps_a[:, :])
                else:
                    p.add(yac[:, m * TC:(m + 1) * TC], ps_a[:, :],
                          yac[:, m * TC:(m + 1) * TC])

        for g in range(16):
            p.dma("sync", scr[:, :], x1s[g * 128:(g + 1) * 128, :])
            p.add(yac[:, g * TC:(g + 1) * TC], yac[:, g * TC:(g + 1) * TC],
                  scr[:, :])
        y_tiles = [yac[:, g * TC:(g + 1) * TC] for g in range(16)]
        _ln_ct(p, y_tiles, y_tiles, ones_col, ones_row, ps_s1, ps_s2, ps_bc,
               sq_scr, lnr, rstd_bc, mr_bc, H, TC)
        for g in range(16):
            p.dma("gpsimd", x2_h[g * 128:(g + 1) * 128, :], y_tiles[g])
        p.build(block)
    return nc



def build_C():
    nc = bass.Bass("TRN2", target_bir_lowering=False)
    D, N, R = 4096, 16, 128
    TCH = TC + 3
    x2h = nc.dram_tensor("x2h", [H, TCH], F32, kind="ExternalInput")
    wipT = nc.dram_tensor("wipT", [H, 2 * D], F32, kind="ExternalInput")
    wxpT = nc.dram_tensor("wxpT", [D, R + 2 * N], F32, kind="ExternalInput")
    wdtT = nc.dram_tensor("wdtT", [R, D], F32, kind="ExternalInput")
    dtb = nc.dram_tensor("dtb", [D, 1], F32, kind="ExternalInput")
    cw4 = nc.dram_tensor("cw4", [D, 4], F32, kind="ExternalInput")
    cb4 = nc.dram_tensor("cb4", [D, 1], F32, kind="ExternalInput")
    u_h = nc.dram_tensor("u", [D, TC], F32, kind="ExternalOutput")
    dt_h = nc.dram_tensor("dt", [D, TC], F32, kind="ExternalOutput")
    sg_h = nc.dram_tensor("sg", [D, TC], F32, kind="ExternalOutput")
    Bm_h = nc.dram_tensor("Bm", [N, TC], F32, kind="ExternalOutput")
    Cm_h = nc.dram_tensor("Cm", [N, TC], F32, kind="ExternalOutput")

    with contextlib.ExitStack() as st:
        e = st.enter_context
        xf = e(nc.sbuf_tensor("xf", [128, TCH], F32))
        xb = e(nc.sbuf_tensor("xb", [128, 16 * TCH], BF16))
        wscr = e(nc.sbuf_tensor("wscr", [128, 2048], F32))
        wslab = e(nc.sbuf_tensor("wslab", [128, 2048], BF16))
        win = e(nc.sbuf_tensor("win", [128, TCH], F32))
        ut = e(nc.sbuf_tensor("ut", [128, TC], F32))
        ub = e(nc.sbuf_tensor("ub", [128, 32 * TC], BF16))
        cw_s = e(nc.sbuf_tensor("cw_s", [128, 32 * 4], F32))
        cb_s = e(nc.sbuf_tensor("cb_s", [128, 32], F32))
        dtb_s = e(nc.sbuf_tensor("dtb_s", [128, 32], F32))
        tsb = e(nc.sbuf_tensor("tsb", [128, TC], BF16))
        bc32 = e(nc.sbuf_tensor("bc32", [32, TC], F32))
        wdt_b = e(nc.sbuf_tensor("wdt_b", [128, D], BF16))
        sp0 = e(nc.sbuf_tensor("sp0", [128, TC], F32))
        sp1 = e(nc.sbuf_tensor("sp1", [128, TC], F32))
        sp2 = e(nc.sbuf_tensor("sp2", [128, TC], F32))
        ps_a = e(nc.psum_tensor("ps_a", [128, TC], F32))
        ps_h = e(nc.psum_tensor("ps_h", [128, 16], F32))
        ps_c = e(nc.psum_tensor("ps_c", [128, TC], F32))
        sem = e(nc.semaphore("G"))
        block = e(nc.Block())

        p = Prog(nc, sem)
        xr = x2h[:, :].rearrange("(g p) t -> p g t", p=128)
        for g in range(16):
            p.dma("sync", xf[:, 0:TCH], xr[:, g, :])
            p.vcopy(xb[:, g * TCH:(g + 1) * TCH], xf[:, 0:TCH])
        p.dma("sync", cw_s[:, :].rearrange("p (k o) -> p k o", k=32),
              cw4[:, :].rearrange("(k p) o -> p k o", p=128))
        _load_vec(p, cb_s, cb4)
        _load_vec(p, dtb_s, dtb)
        _load_wslab(p, wscr, wdt_b, wdtT[:, :])

        for m in range(32):
            _load_wslab(p, wscr, wslab, wipT[:, m * 128:(m + 1) * 128])
            for k in range(16):
                p.mm(ps_a[:, :], wslab[:, k * 128:(k + 1) * 128],
                     xb[:, k * TCH:k * TCH + TC],
                     start=(k == 0), stop=(k == 15))
            for k in range(16):
                p.mm(ps_h[:, 0:3], wslab[:, k * 128:(k + 1) * 128],
                     xb[:, k * TCH + TC:(k + 1) * TCH],
                     start=(k == 0), stop=(k == 15))
            p.copy(win[:, 0:TC], ps_a[:, :])
            p.copy(win[:, TC:TCH], ps_h[:, 0:3])
            p.ts(ut[:, :], win[:, 0:TC], cw_s[:, 4 * m:4 * m + 1])
            for i in range(1, 4):
                p.stt(ut[:, :], win[:, i:i + TC],
                      cw_s[:, 4 * m + i:4 * m + i + 1], ut[:, :],
                      ALU.mult, ALU.add)
            p.ts(ut[:, :], ut[:, :], cb_s[:, m:m + 1], op0=ALU.add)
            p.act(ut[:, :], ut[:, :], AF.Silu)
            p.dma("gpsimd", u_h[m * 128:(m + 1) * 128, :], ut[:, :])
            p.copy(ub[:, m * TC:(m + 1) * TC], ut[:, :])
        for m in range(32):
            _load_wslab(p, wscr, wslab,
                        wipT[:, D + m * 128:D + (m + 1) * 128])
            for k in range(16):
                p.mm(ps_a[:, :], wslab[:, k * 128:(k + 1) * 128],
                     xb[:, k * TCH + 3:(k + 1) * TCH],
                     start=(k == 0), stop=(k == 15))
            p.act(ut[:, :], ps_a[:, :], AF.Silu)
            p.dma("gpsimd", sg_h[m * 128:(m + 1) * 128, :], ut[:, :])
        for k in range(32):
            _load_wslab(p, wscr, wslab, wxpT[k * 128:(k + 1) * 128, :])
            p.mm(ps_a[:, :], wslab[:, 0:128],
                 ub[:, k * TC:(k + 1) * TC], start=(k == 0), stop=(k == 31),
                 sync=True)
            p.mm(ps_c[0:32, :], wslab[:, 128:160],
                 ub[:, k * TC:(k + 1) * TC], start=(k == 0), stop=(k == 31),
                 sync=True)
        p.copy(sp0[:, :], ps_a[:, :])
        p.vcopy(bc32[:, :], ps_c[0:32, :])
        p.dma("gpsimd", Bm_h[:, :], bc32[0:16, :])
        p.dma("gpsimd", Cm_h[:, :], bc32[16:32, :])
        p.copy(tsb[:, :], sp0[:, :])
        for m in range(32):
            p.mm(ps_a[:, :], wdt_b[:, m * 128:(m + 1) * 128], tsb[:, :],
                 start=True, stop=True)
            p.ts(sp0[:, :], ps_a[:, :], dtb_s[:, m:m + 1], op0=ALU.add)
            _softplus(p, sp0[:, :], sp0[:, :], sp1[:, :], sp2[:, :])
            p.dma("gpsimd", dt_h[m * 128:(m + 1) * 128, :], sp0[:, :])
        p.build(block)
    return nc



def build_D():
    nc = bass.Bass("TRN2", target_bir_lowering=False)
    DC, N = 512, 16
    TD = 1024
    ND = S // TD
    u_h = nc.dram_tensor("u", [DC, S], F32, kind="ExternalInput")
    dt_h = nc.dram_tensor("dt", [DC, S], F32, kind="ExternalInput")
    sg_h = nc.dram_tensor("sg", [DC, S], F32, kind="ExternalInput")
    Bm_h = nc.dram_tensor("Bm", [N, S], F32, kind="ExternalInput")
    Cm_h = nc.dram_tensor("Cm", [N, S], F32, kind="ExternalInput")
    Alog = nc.dram_tensor("Alog", [DC, N], F32, kind="ExternalInput")
    Dp = nc.dram_tensor("Dp", [DC, 1], F32, kind="ExternalInput")
    sel_h = nc.dram_tensor("sel16", [N, N * 128], F32, kind="ExternalInput")
    y_h = nc.dram_tensor("y", [DC, S], F32, kind="ExternalOutput")

    with contextlib.ExitStack() as st:
        e = st.enter_context
        A_s = e(nc.sbuf_tensor("A_s", [128, 4 * N], F32))
        Dp_s = e(nc.sbuf_tensor("Dp_s", [128, 4], F32))
        sels = e(nc.sbuf_tensor("sels", [N, N * 128], F32))
        uf = e(nc.sbuf_tensor("uf", [128, 4 * TD], F32))
        dtf = e(nc.sbuf_tensor("dtf", [128, 4 * TD], F32))
        du = e(nc.sbuf_tensor("du", [128, 4 * TD], F32))
        yt = e(nc.sbuf_tensor("yt", [128, 4 * TD], F32))
        bm16 = e(nc.sbuf_tensor("bm16", [N, TD], F32))
        cm16 = e(nc.sbuf_tensor("cm16", [N, TD], F32))
        bbc = e(nc.sbuf_tensor("bbc", [128, TD], F32))
        cbc = e(nc.sbuf_tensor("cbc", [128, TD], F32))
        da = e(nc.sbuf_tensor("da", [128, TD], F32))
        dbu = e(nc.sbuf_tensor("dbu", [128, TD], F32))
        hsc = e(nc.sbuf_tensor("hsc", [128, TD], F32))
        tmp = e(nc.sbuf_tensor("tmp", [128, TD], F32))
        sgt = e(nc.sbuf_tensor("sgt", [128, TD], F32))
        carry = e(nc.sbuf_tensor("carry", [128, 64], F32))
        ps_a = e(nc.psum_tensor("ps_a", [128, TC], F32))
        sem = e(nc.semaphore("G"))
        block = e(nc.Block())

        p = Prog(nc, sem)
        p.dma("sync", sels[:, :], sel_h[:, :])
        p.dma("sync", A_s[:, :].rearrange("p (k n) -> p k n", k=4),
              Alog[:, :].rearrange("(k p) n -> p k n", p=128))
        p.act(A_s[:, :], A_s[:, :], AF.Exp)
        p.ts(A_s[:, :], A_s[:, :], -1.0)
        _load_vec(p, Dp_s, Dp)

        for it in range(ND):
            t0 = it * TD
            for d in range(4):
                p.dma("sync", uf[:, d * TD:(d + 1) * TD],
                      u_h[d * 128:(d + 1) * 128, t0:t0 + TD])
                p.dma("sync", dtf[:, d * TD:(d + 1) * TD],
                      dt_h[d * 128:(d + 1) * 128, t0:t0 + TD])
                p.mul(du[:, d * TD:(d + 1) * TD], dtf[:, d * TD:(d + 1) * TD],
                      uf[:, d * TD:(d + 1) * TD])
            p.dma("sync", bm16[:, :], Bm_h[:, t0:t0 + TD])
            p.dma("sync", cm16[:, :], Cm_h[:, t0:t0 + TD])
            for n in range(N):
                for half in range(2):
                    sl = slice(half * TC, (half + 1) * TC)
                    p.mm(ps_a[:, :], sels[:, n * 128:(n + 1) * 128],
                         bm16[:, sl], start=True, stop=True)
                    p.copy(bbc[:, sl], ps_a[:, :])
                    p.mm(ps_a[:, :], sels[:, n * 128:(n + 1) * 128],
                         cm16[:, sl], start=True, stop=True)
                    p.copy(cbc[:, sl], ps_a[:, :])
                for d in range(4):
                    dsl = slice(d * TD, (d + 1) * TD)
                    ci = n * 4 + d
                    p.act(da[:, :], dtf[:, dsl], AF.Exp,
                          scale=A_s[:, d * N + n:d * N + n + 1])
                    p.mul(dbu[:, :], du[:, dsl], bbc[:, :])
                    if it == 0:
                        p.scan(hsc[:, :], da[:, :], dbu[:, :], 0.0)
                    else:
                        p.scan(hsc[:, :], da[:, :], dbu[:, :],
                               carry[:, ci:ci + 1])
                    p.vcopy(carry[:, ci:ci + 1], hsc[:, TD - 1:TD])
                    p.mul(tmp[:, :], hsc[:, :], cbc[:, :])
                    if n == 0:
                        p.vcopy(yt[:, dsl], tmp[:, :])
                    else:
                        p.add(yt[:, dsl], yt[:, dsl], tmp[:, :])
            for d in range(4):
                dsl = slice(d * TD, (d + 1) * TD)
                p.stt(yt[:, dsl], uf[:, dsl], Dp_s[:, d:d + 1], yt[:, dsl],
                      ALU.mult, ALU.add)
                p.dma("sync", sgt[:, :], sg_h[d * 128:(d + 1) * 128,
                                              t0:t0 + TD])
                p.mul(yt[:, dsl], yt[:, dsl], sgt[:, :])
                p.dma("gpsimd", y_h[d * 128:(d + 1) * 128, t0:t0 + TD],
                      yt[:, dsl])
        p.build(block)
    return nc



def build_E():
    nc = bass.Bass("TRN2", target_bir_lowering=False)
    D, INTER = 4096, 5632
    NI = INTER // 128
    yc = nc.dram_tensor("yc", [D, TC], F32, kind="ExternalInput")
    x2c = nc.dram_tensor("x2c", [H, TC], F32, kind="ExternalInput")
    wopT = nc.dram_tensor("wopT", [D, H], F32, kind="ExternalInput")
    gwT = nc.dram_tensor("gwT", [H, INTER], F32, kind="ExternalInput")
    uwT = nc.dram_tensor("uwT", [H, INTER], F32, kind="ExternalInput")
    dwT = nc.dram_tensor("dwT", [INTER, H], F32, kind="ExternalInput")
    gs = nc.dram_tensor("gs", [INTER, 1], F32, kind="ExternalInput")
    us = nc.dram_tensor("us", [INTER, 1], F32, kind="ExternalInput")
    ds = nc.dram_tensor("ds", [H, 1], F32, kind="ExternalInput")
    out_h = nc.dram_tensor("out", [H, TC], F32, kind="ExternalOutput")
    x3s = nc.dram_tensor("x3s", [H, TC], F32)

    with contextlib.ExitStack() as st:
        e = st.enter_context
        yb = e(nc.sbuf_tensor("yb", [128, 32 * TC], BF16))
        scr = e(nc.sbuf_tensor("scr", [128, TC], F32))
        rt = e(nc.sbuf_tensor("rt", [128, 16 * TC], F32))
        x3b = e(nc.sbuf_tensor("x3b", [128, 16 * TC], BF16))
        hdn = e(nc.sbuf_tensor("hdn", [128, NI * TC], BF16))
        gt = e(nc.sbuf_tensor("gt", [128, TC], F32))
        ut2 = e(nc.sbuf_tensor("ut2", [128, TC], F32))
        wscr = e(nc.sbuf_tensor("wscr", [128, 2048], F32))
        wslab = e(nc.sbuf_tensor("wslab", [128, 4096], BF16))
        w3slab = e(nc.sbuf_tensor("w3slab", [128, NI * 128], BF16))
        gs_s = e(nc.sbuf_tensor("gs_s", [128, NI], F32))
        us_s = e(nc.sbuf_tensor("us_s", [128, NI], F32))
        ds_s = e(nc.sbuf_tensor("ds_s", [128, 16], F32))
        sq_scr = e(nc.sbuf_tensor("sq_scr", [128, TC], F32))
        ones_col = e(nc.sbuf_tensor("ones_col", [128, 1], F32))
        ones_row = e(nc.sbuf_tensor("ones_row", [1, 128], F32))
        rstd_bc = e(nc.sbuf_tensor("rstd_bc", [128, TC], F32))
        mr_bc = e(nc.sbuf_tensor("mr_bc", [128, TC], F32))
        lnr = e(nc.sbuf_tensor("lnr", [1, 5 * TC], F32))
        ps_a = e(nc.psum_tensor("ps_a", [128, TC], F32))
        ps_s1 = e(nc.psum_tensor("ps_s1", [128, TC], F32))
        ps_s2 = e(nc.psum_tensor("ps_s2", [128, TC], F32))
        ps_bc = e(nc.psum_tensor("ps_bc", [128, TC], F32))
        sem = e(nc.semaphore("G"))
        block = e(nc.Block())

        p = Prog(nc, sem)
        p.memset(ones_col[:, :], 1.0)
        p.memset(ones_row[:, :], 1.0)
        _load_vec(p, gs_s, gs)
        _load_vec(p, us_s, us)
        _load_vec(p, ds_s, ds)

        for g in range(32):
            p.dma("sync", scr[:, :], yc[g * 128:(g + 1) * 128, :])
            p.vcopy(yb[:, g * TC:(g + 1) * TC], scr[:, :])
        for m in range(16):
            _load_wslab(p, wscr, wslab, wopT[:, m * 128:(m + 1) * 128])
            for k in range(32):
                p.mm(ps_a[:, :], wslab[:, k * 128:(k + 1) * 128],
                     yb[:, k * TC:(k + 1) * TC],
                     start=(k == 0), stop=(k == 31))
            p.dma("sync", scr[:, :], x2c[m * 128:(m + 1) * 128, :])
            p.add(rt[:, m * TC:(m + 1) * TC], ps_a[:, :], scr[:, :])
        r_tiles = [rt[:, g * TC:(g + 1) * TC] for g in range(16)]
        _ln_ct(p, r_tiles, r_tiles, ones_col, ones_row, ps_s1, ps_s2, ps_bc,
               sq_scr, lnr, rstd_bc, mr_bc, H, TC)
        for g in range(16):
            p.copy(x3b[:, g * TC:(g + 1) * TC], r_tiles[g])
            p.dma("gpsimd", x3s[g * 128:(g + 1) * 128, :], r_tiles[g])

        for ig in range(NI):
            _load_wslab(p, wscr, wslab, gwT[:, ig * 128:(ig + 1) * 128])
            for k in range(16):
                p.mm(ps_a[:, :], wslab[:, k * 128:(k + 1) * 128],
                     x3b[:, k * TC:(k + 1) * TC],
                     start=(k == 0), stop=(k == 15))
            p.ts(gt[:, :], ps_a[:, :], gs_s[:, ig:ig + 1])
            p.act(gt[:, :], gt[:, :], AF.Silu)
            _load_wslab(p, wscr, wslab, uwT[:, ig * 128:(ig + 1) * 128])
            for k in range(16):
                p.mm(ps_a[:, :], wslab[:, k * 128:(k + 1) * 128],
                     x3b[:, k * TC:(k + 1) * TC],
                     start=(k == 0), stop=(k == 15))
            p.ts(ut2[:, :], ps_a[:, :], us_s[:, ig:ig + 1])
            p.mul(hdn[:, ig * TC:(ig + 1) * TC], gt[:, :], ut2[:, :])
        for m in range(16):
            _load_wslab(p, wscr, w3slab, dwT[:, m * 128:(m + 1) * 128])
            for ig in range(NI):
                p.mm(ps_a[:, :], w3slab[:, ig * 128:(ig + 1) * 128],
                     hdn[:, ig * TC:(ig + 1) * TC],
                     start=(ig == 0), stop=(ig == NI - 1))
            p.dma("sync", scr[:, :], x3s[m * 128:(m + 1) * 128, :])
            p.stt(rt[:, m * TC:(m + 1) * TC], ps_a[:, :],
                  ds_s[:, m:m + 1], scr[:, :], ALU.mult, ALU.add)
        _ln_ct(p, r_tiles, r_tiles, ones_col, ones_row, ps_s1, ps_s2, ps_bc,
               sq_scr, lnr, rstd_bc, mr_bc, H, TC)
        for g in range(16):
            p.dma("gpsimd", out_h[g * 128:(g + 1) * 128, :], r_tiles[g])
        p.build(block)
    return nc



_PROGS = {}


def _prog(name, builder):
    if name not in _PROGS:
        _PROGS[name] = builder()
    return _PROGS[name]


def _ct(a):
    return np.ascontiguousarray(a)


def _eye_sel(n):
    sel = np.zeros((n, n * 128), np.float32)
    for i in range(n):
        sel[i, i * 128:(i + 1) * 128] = 1.0
    return sel


def run_A(inp, xT):
    f32 = np.float32
    W_xy = np.asarray(inp["L0_attn_W_xy"], f32)
    ig_w = np.asarray(inp["L0_attn_ig_w"], f32)
    ig_b = np.asarray(inp["L0_attn_ig_b"], f32)
    ag_w = np.asarray(inp["L0_attn_ag_w"], f32)
    ag_b = np.asarray(inp["L0_attn_ag_b"], f32)
    a_par = np.asarray(inp["L0_attn_a_param"], f32)
    conv_w = np.asarray(inp["L0_attn_conv_w"], f32)
    conv_b = np.asarray(inp["L0_attn_conv_b"], f32)
    ncA = _prog("A", build_A)
    in_maps = []
    for c in range(NCORE):
        r0 = c * 256
        wsl = np.concatenate([W_xy[r0:r0 + 256], W_xy[H + r0:H + r0 + 256]],
                             axis=0)
        in_maps.append({
            "xT": xT,
            "wxyT": _ct(wsl.T),
            "igw": _ct(ig_w[2 * c:2 * c + 2].reshape(256, 128)),
            "igb": _ct(ig_b[2 * c:2 * c + 2].reshape(256, 1)),
            "agw": _ct(ag_w[2 * c:2 * c + 2].reshape(256, 128)),
            "agb": _ct(ag_b[2 * c:2 * c + 2].reshape(256, 1)),
            "apar": _ct(a_par[r0:r0 + 256].reshape(256, 1)),
            "cw": _ct(conv_w[:, r0:r0 + 256].T),
            "cb": _ct(conv_b[r0:r0 + 256].reshape(256, 1)),
        })
    res = run_bass_kernel_spmd(ncA, in_maps, core_ids=list(range(NCORE)))
    return np.concatenate([res.results[c]["z"] for c in range(NCORE)], axis=0)


def run_B(inp, xT, zT):
    f32 = np.float32
    wresT = _ct(np.asarray(inp["L0_attn_W_resid"], f32).T)
    gwT0 = _ct(np.asarray(inp["L0_ffn_gate_w"], f32).T)
    w1T = _ct(np.asarray(inp["L0_ffn_w1"], f32).transpose(0, 2, 1))
    w2T = _ct(np.asarray(inp["L0_ffn_w2"], f32).transpose(0, 2, 1))
    w3T = _ct(np.asarray(inp["L0_ffn_w3"], f32).transpose(0, 2, 1))
    sw1T = _ct(np.asarray(inp["L0_ffn_sw1"], f32).T)
    sw2T = _ct(np.asarray(inp["L0_ffn_sw2"], f32).T)
    sw3T = _ct(np.asarray(inp["L0_ffn_sw3"], f32).T)
    eye = np.eye(128, dtype=f32)
    sel8 = _eye_sel(8)
    ncB = _prog("B", build_B)
    x2T = np.empty((H, S), f32)
    for ch in range(4):
        in_maps = []
        for c in range(NCORE):
            t0 = c * 2048 + ch * TC
            in_maps.append({
                "zc": _ct(zT[:, t0:t0 + TC]),
                "xcd": _ct(xT[:, t0:t0 + TC]),
                "wresT": wresT, "gwT": gwT0,
                "w1T": w1T, "w2T": w2T, "w3T": w3T,
                "sw1T": sw1T, "sw2T": sw2T, "sw3T": sw3T,
                "eye": eye, "sel8": sel8,
            })
        res = run_bass_kernel_spmd(ncB, in_maps, core_ids=list(range(NCORE)))
        for c in range(NCORE):
            t0 = c * 2048 + ch * TC
            x2T[:, t0:t0 + TC] = res.results[c]["x2"]
    return x2T


def run_C(inp, x2T):
    f32 = np.float32
    wipT = _ct(np.asarray(inp["L1_attn_in_proj_w"], f32).T)
    wxpT = _ct(np.asarray(inp["L1_attn_x_proj_w"], f32).T)
    wdtT = _ct(np.asarray(inp["L1_attn_dt_proj_w"], f32).T)
    dtb = _ct(np.asarray(inp["L1_attn_dt_proj_b"], f32).reshape(4096, 1))
    cw4 = _ct(np.asarray(inp["L1_attn_conv_w"], f32).reshape(4096, 4))
    cb4 = _ct(np.asarray(inp["L1_attn_conv_b"], f32).reshape(4096, 1))
    ncC = _prog("C", build_C)
    u_f = np.empty((4096, S), f32)
    dt_f = np.empty((4096, S), f32)
    sg_f = np.empty((4096, S), f32)
    Bm_f = np.empty((16, S), f32)
    Cm_f = np.empty((16, S), f32)
    x2T_pad = np.concatenate([np.zeros((H, 3), f32), x2T], axis=1)
    for ch in range(4):
        in_maps = []
        for c in range(NCORE):
            t0 = c * 2048 + ch * TC
            in_maps.append({
                "x2h": _ct(x2T_pad[:, t0:t0 + TC + 3]),
                "wipT": wipT, "wxpT": wxpT, "wdtT": wdtT,
                "dtb": dtb, "cw4": cw4, "cb4": cb4,
            })
        res = run_bass_kernel_spmd(ncC, in_maps, core_ids=list(range(NCORE)))
        for c in range(NCORE):
            t0 = c * 2048 + ch * TC
            u_f[:, t0:t0 + TC] = res.results[c]["u"]
            dt_f[:, t0:t0 + TC] = res.results[c]["dt"]
            sg_f[:, t0:t0 + TC] = res.results[c]["sg"]
            Bm_f[:, t0:t0 + TC] = res.results[c]["Bm"]
            Cm_f[:, t0:t0 + TC] = res.results[c]["Cm"]
    return u_f, dt_f, sg_f, Bm_f, Cm_f


def run_D(inp, u_f, dt_f, sg_f, Bm_f, Cm_f):
    f32 = np.float32
    Alog = np.asarray(inp["L1_attn_A_log"], f32)
    Dp = np.asarray(inp["L1_attn_D_param"], f32).reshape(4096, 1)
    sel16 = _eye_sel(16)
    ncD = _prog("D", build_D)
    in_maps = []
    for c in range(NCORE):
        d0 = c * 512
        in_maps.append({
            "u": _ct(u_f[d0:d0 + 512]), "dt": _ct(dt_f[d0:d0 + 512]),
            "sg": _ct(sg_f[d0:d0 + 512]),
            "Bm": Bm_f, "Cm": Cm_f,
            "Alog": _ct(Alog[d0:d0 + 512]), "Dp": _ct(Dp[d0:d0 + 512]),
            "sel16": sel16,
        })
    res = run_bass_kernel_spmd(ncD, in_maps, core_ids=list(range(NCORE)))
    return np.concatenate([res.results[c]["y"] for c in range(NCORE)], axis=0)


def run_E(inp, x2T, y_f):
    f32 = np.float32
    wopT = _ct(np.asarray(inp["L1_attn_out_proj_w"], f32).T)
    gwT1 = _ct(np.asarray(inp["L1_ffn_gate_w"], f32).T)
    uwT = _ct(np.asarray(inp["L1_ffn_up_w"], f32).T)
    dwT = _ct(np.asarray(inp["L1_ffn_down_w"], f32).T)
    gs = _ct(np.asarray(inp["L1_ffn_gate_s"], f32).reshape(-1, 1))
    us = _ct(np.asarray(inp["L1_ffn_up_s"], f32).reshape(-1, 1))
    ds = _ct(np.asarray(inp["L1_ffn_down_s"], f32).reshape(-1, 1))
    ncE = _prog("E", build_E)
    outT = np.empty((H, S), f32)
    for ch in range(4):
        in_maps = []
        for c in range(NCORE):
            t0 = c * 2048 + ch * TC
            in_maps.append({
                "yc": _ct(y_f[:, t0:t0 + TC]),
                "x2c": _ct(x2T[:, t0:t0 + TC]),
                "wopT": wopT, "gwT": gwT1, "uwT": uwT, "dwT": dwT,
                "gs": gs, "us": us, "ds": ds,
            })
        res = run_bass_kernel_spmd(ncE, in_maps, core_ids=list(range(NCORE)))
        for c in range(NCORE):
            t0 = c * 2048 + ch * TC
            outT[:, t0:t0 + TC] = res.results[c]["out"]
    return outT


def kernel(**inp):
    return kernel_fused(inp)


def kernel_unfused(**inp):
    x = np.asarray(inp["L0_attn_x"], np.float32).reshape(S, H)
    xT = _ct(x.T)
    zT = run_A(inp, xT)
    x2T = run_B(inp, xT, zT)
    u_f, dt_f, sg_f, Bm_f, Cm_f = run_C(inp, x2T)
    y_f = run_D(inp, u_f, dt_f, sg_f, Bm_f, Cm_f)
    outT = run_E(inp, x2T, y_f)
    return _ct(outT.T).reshape(1, S, H).astype(np.float32)



FDEBUG = False


class Pool:
    def __init__(self, nc, st, name, n_f32):
        self.h = st.enter_context(nc.sbuf_tensor(name, [128, n_f32], F32))
        self.bview = self.h[:, :].bitcast(BF16)
        self.n = n_f32
        self.off = 0

    def f32(self, cols):
        o = self.off
        self.off = o + ((cols + 15) // 16) * 16
        assert self.off <= self.n, f"pool overflow {self.off} > {self.n}"
        return self.h[:, o:o + cols]

    def bf16(self, cols):
        cf = (cols + 1) // 2
        o = self.off
        self.off = o + ((cf + 15) // 16) * 16
        assert self.off <= self.n, f"pool overflow {self.off} > {self.n}"
        return self.bview[:, 2 * o:2 * o + cols]

    def reset(self):
        self.off = 0


def _collective(p, kind, ins, outs, sem):
    def fn(nc):
        with allow_ici_all_to_all():
            nc.gpsimd.collective_compute(
                kind, ALU.bypass, replica_groups=[list(range(NCORE))],
                ins=[a.opt() for a in ins], outs=[a.opt() for a in outs],
            ).then_inc(sem)
        nc.gpsimd.wait_ge(sem, 1)
        return nc.gpsimd.nop()
    p.step("gpsimd", fn)


def _emit_A(p, nc, pool, ps, hd):
    RC, M2 = 256, 512
    NT = S // TC
    wscr = pool.f32(2048)
    wxy_b = pool.bf16(16 * M2)
    igw_b = pool.bf16(2 * 128)
    agw_b = pool.bf16(2 * 128)
    sm = pool.f32(16)
    igb_s, agb_s, apar_s, cb_s, sp_s = (sm[:, 0:2], sm[:, 2:4], sm[:, 4:6],
                                        sm[:, 6:8], sm[:, 8:10])
    t1s, t2s = sm[:, 10:11], sm[:, 11:12]
    cw_s = pool.f32(8)
    xf = pool.f32(16 * TC)
    xb = pool.bf16(16 * TC)
    win = pool.f32(2 * (TC + 3))
    xc = pool.f32(2 * TC)
    xcb = pool.bf16(2 * TC)
    gy = pool.f32(2 * TC)
    g1 = pool.f32(TC)
    g2 = pool.f32(TC)
    s1 = pool.f32(TC)
    s3 = pool.f32(TC)
    av = pool.f32(TC)
    nx = pool.f32(TC)
    hb = pool.f32(2 * TC)
    zt = pool.f32(TC)
    carry = pool.f32(2)

    _load_wslab(p, wscr, wxy_b, hd["wxyT"][:, :])
    _load_wslab(p, wscr, igw_b, hd["igw"][:, :])
    _load_wslab(p, wscr, agw_b, hd["agw"][:, :])
    for (v_h, dst) in (("igb", igb_s), ("agb", agb_s), ("apar", apar_s),
                       ("cb", cb_s)):
        _load_vec(p, dst, hd[v_h][:, :])
    p.dma("sync", cw_s.rearrange("p (k o) -> p k o", k=2),
          hd["cw"][:, :].rearrange("(k p) o -> p k o", p=128))
    for g in range(2):
        _softplus(p, sp_s[:, g:g + 1], apar_s[:, g:g + 1], t1s, t2s)

    za_in = hd["za_in"]
    for it in range(NT):
        t0 = it * TC
        dblk = t0 // 2048
        off = t0 % 2048
        xr = hd["xT"][:, t0:t0 + TC].rearrange("(g p) t -> p g t", p=128)
        for i in range(4):
            p.dma("sync", xf[:, i * 4 * TC:(i + 1) * 4 * TC].rearrange(
                "p (g t) -> p g t", g=4), xr[:, i * 4:(i + 1) * 4, :])
        for g in range(16):
            p.copy(xb[:, g * TC:(g + 1) * TC], xf[:, g * TC:(g + 1) * TC])
        for m in range(4):
            for k in range(16):
                p.mm(ps["a"][:, :],
                     wxy_b[:, k * M2 + m * 128:k * M2 + (m + 1) * 128],
                     xb[:, k * TC:(k + 1) * TC],
                     start=(k == 0), stop=(k == 15))
            if m < 2:
                w0 = m * (TC + 3)
                p.copy(win[:, w0 + 3:w0 + TC + 3], ps["a"][:, :])
            else:
                g = m - 2
                p.act(gy[:, g * TC:(g + 1) * TC], ps["a"][:, :], AF.Gelu)
        for g in range(2):
            w0 = g * (TC + 3)
            if it == 0:
                p.memset(win[:, w0:w0 + 3], 0.0)
            xcg = xc[:, g * TC:(g + 1) * TC]
            p.ts(xcg, win[:, w0:w0 + TC], cw_s[:, 4 * g:4 * g + 1])
            for i in range(1, 4):
                p.stt(xcg, win[:, w0 + i:w0 + i + TC],
                      cw_s[:, 4 * g + i:4 * g + i + 1], xcg,
                      ALU.mult, ALU.add)
            p.ts(xcg, xcg, cb_s[:, g:g + 1], op0=ALU.add)
            p.vcopy(win[:, w0:w0 + 3], win[:, w0 + TC:w0 + TC + 3])
            p.copy(xcb[:, g * TC:(g + 1) * TC], xcg)
            xcbg = xcb[:, g * TC:(g + 1) * TC]
            p.mm(ps["a"][:, :], igw_b[:, g * 128:(g + 1) * 128], xcbg,
                 start=True, stop=True)
            p.ts(g1[:, :], ps["a"][:, :], igb_s[:, g:g + 1], op0=ALU.add)
            p.mm(ps["a"][:, :], agw_b[:, g * 128:(g + 1) * 128], xcbg,
                 start=True, stop=True)
            p.ts(g2[:, :], ps["a"][:, :], agb_s[:, g:g + 1], op0=ALU.add)
            p.act(s1[:, :], g1[:, :], AF.Sigmoid)
            p.act(g2[:, :], g2[:, :], AF.Sigmoid)
            p.ts(g2[:, :], g2[:, :], sp_s[:, g:g + 1], -8.0)
            p.act(av[:, :], g2[:, :], AF.Exp)
            p.act(s3[:, :], av[:, :], AF.Square)
            p.act(s3[:, :], s3[:, :], AF.Sqrt, bias=1.0, scale=-1.0)
            p.mul(nx[:, :], xcg, s1[:, :])
            p.mul(nx[:, :], nx[:, :], s3[:, :])
            hg = hb[:, g * TC:(g + 1) * TC]
            if it == 0:
                p.scan(hg, av[:, :], nx[:, :], 0.0)
            else:
                p.scan(hg, av[:, :], nx[:, :], carry[:, g:g + 1])
            p.vcopy(carry[:, g:g + 1], hg[:, TC - 1:TC])
            p.mul(zt[:, :], gy[:, g * TC:(g + 1) * TC], hg)
            r0 = dblk * 256 + g * 128
            p.dma("gpsimd", za_in[r0:r0 + 128, off:off + TC], zt[:, :])


def _emit_B_chunk(p, nc, pool, ps, hd, ch):
    E, I = 8, 1408
    NI = I // 128
    zc = hd["za_out"][:, ch * TC:(ch + 1) * TC]
    xcd = hd["xcore"][:, ch * TC:(ch + 1) * TC]
    x1s = hd["x1s"][:, ch * TC:(ch + 1) * TC].rearrange(
        "p t -> p t") if False else hd["x1s"]
    x2o = hd["x2sc"]
    eyes, sels = hd["_eyes"], hd["_sels8"]
    ones_col, ones_row = hd["_ones_col"], hd["_ones_row"]

    scr = pool.f32(TC)
    ru = pool.f32(16 * TC)
    rt = ru
    yac = ru
    wscr = pool.f32(2048)
    sq_scr = pool.f32(TC)
    rstd_bc = pool.f32(TC)
    mr_bc = pool.f32(TC)
    lnr_t = pool.f32(5 * TC)
    lnr = lnr_t[0:1, :]
    comb8_t = pool.f32(TC)
    comb8 = comb8_t[0:8, :]
    lt = pool.f32(32)
    eqt = pool.f32(8)
    ext = pool.f32(8)
    rcol = pool.f32(8)
    x1b = pool.bf16(16 * TC)
    wslab = pool.bf16(2048)
    gw_b = pool.bf16(16 * E)
    comb_bc = pool.bf16(TC)
    un = pool.bf16(16 * TC)
    zb = un
    ht = un[:, 0:NI * TC]
    h1b = un[:, NI * TC:NI * TC + TC]
    h2b = un[:, NI * TC + TC:NI * TC + 2 * TC]
    w3slab = un[:, NI * TC + 2 * TC:NI * TC + 2 * TC + NI * 128]

    for g in range(16):
        p.dma("sync", scr[:, :], zc[g * 128:(g + 1) * 128, :])
        p.vcopy(zb[:, g * TC:(g + 1) * TC], scr[:, :])
    for m in range(16):
        _load_wslab(p, wscr, wslab, hd["wresT"][:, m * 128:(m + 1) * 128])
        for k in range(16):
            p.mm(ps["a"][:, :], wslab[:, k * 128:(k + 1) * 128],
                 zb[:, k * TC:(k + 1) * TC], start=(k == 0), stop=(k == 15))
        p.dma("sync", scr[:, :], xcd[m * 128:(m + 1) * 128, :])
        p.add(rt[:, m * TC:(m + 1) * TC], ps["a"][:, :], scr[:, :])
    r_tiles = [rt[:, g * TC:(g + 1) * TC] for g in range(16)]
    _ln_ct(p, r_tiles, r_tiles, ones_col, ones_row, ps["s1"], ps["s2"],
           ps["bc"], sq_scr, lnr, rstd_bc, mr_bc, H, TC)
    for g in range(16):
        p.copy(x1b[:, g * TC:(g + 1) * TC], r_tiles[g])
        p.dma("gpsimd", x1s[g * 128:(g + 1) * 128, ch * TC:(ch + 1) * TC],
              r_tiles[g])
    _load_wslab(p, wscr, gw_b, hd["gwT"][:, :])
    for s in range(4):
        for k in range(16):
            p.mm(ps["a"][:, s * 8:(s + 1) * 8],
                 x1b[:, k * TC + s * 128:k * TC + (s + 1) * 128],
                 gw_b[:, k * E:(k + 1) * E], start=(k == 0), stop=(k == 15))
    p.vcopy(lt[:, :], ps["a"][:, 0:32])
    for s in range(4):
        lts = lt[:, s * 8:(s + 1) * 8]
        m1, nm1, m2 = rcol[:, 0:1], rcol[:, 1:2], rcol[:, 2:3]
        sm2, rs = rcol[:, 3:4], rcol[:, 4:5]
        p.rmax(m1, lts)
        p.ts(nm1, m1, -1.0)
        p.ts(eqt[:, :], lts, m1, op0=ALU.is_equal)
        p.stt(eqt[:, :], eqt[:, :], -1e9, lts, ALU.mult, ALU.add)
        p.rmax(m2, eqt[:, :])
        p.act(ext[:, :], lts, AF.Exp, bias=nm1)
        p.rsum(sm2, ext[:, :])
        p.recip(rs, sm2)
        p.ts(eqt[:, :], lts, m2, op0=ALU.is_ge)
        p.mul(ext[:, :], ext[:, :], eqt[:, :])
        p.ts(ext[:, :], ext[:, :], rs)
        p.transpose(ps["bc"][0:8, 0:128], ext[:, :], eyes[:, :])
        p.vcopy(comb8[:, s * 128:(s + 1) * 128], ps["bc"][0:8, 0:128])
    ves = [("sh", 0), ("sh", 1)] + [("e", i) for i in range(E)]
    for vi, (kind, idx) in enumerate(ves):
        if kind == "e":
            p.mm(ps["bc"][:, :], sels[:, idx * 128:(idx + 1) * 128],
                 comb8[:, :], start=True, stop=True)
            p.copy(comb_bc[:, :], ps["bc"][:, :])
        for ig in range(NI):
            if kind == "sh":
                c0 = idx * I + ig * 128
                a1 = hd["sw1T"][:, c0:c0 + 128]
                a2 = hd["sw2T"][:, c0:c0 + 128]
            else:
                c0 = ig * 128
                a1 = hd["w1T"][idx, :, c0:c0 + 128]
                a2 = hd["w2T"][idx, :, c0:c0 + 128]
            _load_wslab(p, wscr, wslab, a1)
            for k in range(16):
                p.mm(ps["a"][:, :], wslab[:, k * 128:(k + 1) * 128],
                     x1b[:, k * TC:(k + 1) * TC],
                     start=(k == 0), stop=(k == 15))
            p.act(h1b[:, :], ps["a"][:, :], AF.Silu)
            _load_wslab(p, wscr, wslab, a2)
            for k in range(16):
                p.mm(ps["a"][:, :], wslab[:, k * 128:(k + 1) * 128],
                     x1b[:, k * TC:(k + 1) * TC],
                     start=(k == 0), stop=(k == 15))
            p.copy(h2b[:, :], ps["a"][:, :])
            htg = ht[:, ig * TC:(ig + 1) * TC]
            p.mul(htg, h1b[:, :], h2b[:, :])
            if kind == "e":
                p.mul(htg, htg, comb_bc[:, :])
        for m in range(16):
            if kind == "sh":
                a3 = hd["sw3T"][idx * I:(idx + 1) * I, m * 128:(m + 1) * 128]
            else:
                a3 = hd["w3T"][idx, :, m * 128:(m + 1) * 128]
            _load_wslab(p, wscr, w3slab, a3)
            for ig in range(NI):
                p.mm(ps["a"][:, :], w3slab[:, ig * 128:(ig + 1) * 128],
                     ht[:, ig * TC:(ig + 1) * TC],
                     start=(ig == 0), stop=(ig == NI - 1))
            if vi == 0:
                p.copy(yac[:, m * TC:(m + 1) * TC], ps["a"][:, :])
            else:
                p.add(yac[:, m * TC:(m + 1) * TC], ps["a"][:, :],
                      yac[:, m * TC:(m + 1) * TC])
    for g in range(16):
        p.dma("sync", scr[:, :],
              x1s[g * 128:(g + 1) * 128, ch * TC:(ch + 1) * TC])
        p.add(yac[:, g * TC:(g + 1) * TC], yac[:, g * TC:(g + 1) * TC],
              scr[:, :])
    y_tiles = [yac[:, g * TC:(g + 1) * TC] for g in range(16)]
    _ln_ct(p, y_tiles, y_tiles, ones_col, ones_row, ps["s1"], ps["s2"],
           ps["bc"], sq_scr, lnr, rstd_bc, mr_bc, H, TC)
    for g in range(16):
        p.dma("gpsimd", x2o[g * 128:(g + 1) * 128, ch * TC:(ch + 1) * TC],
              y_tiles[g])


def _emit_halo(p, nc, pool, hd):
    gmask = hd["_gmask"]
    t3 = pool.f32(3)
    tm = pool.f32(3)
    acc = pool.f32(3)
    for g in range(16):
        p.memset(acc[:, :], 0.0)
        for s2 in range(NCORE):
            p.dma("sync", t3[:, :],
                  hd["tail_out"][s2 * H + g * 128:s2 * H + (g + 1) * 128, :])
            p.ts(tm[:, :], t3[:, :], gmask[:, s2:s2 + 1])
            p.add(acc[:, :], acc[:, :], tm[:, :])
        p.dma("gpsimd", hd["halo3"][g * 128:(g + 1) * 128, :], acc[:, :])


def _emit_C_chunk(p, nc, pool, ps, hd, ch):
    D, N, R = 4096, 16, 128
    TCH = TC + 3
    x2sc = hd["x2sc"]
    xf = pool.f32(TCH)
    xb = pool.bf16(16 * TCH)
    wscr = pool.f32(2048)
    wslab = pool.bf16(2048)
    win = pool.f32(TCH)
    ut = pool.f32(TC)
    ub = pool.bf16(32 * TC)
    cw_s = pool.f32(32 * 4)
    cb_s = pool.f32(32)
    dtb_s = pool.f32(32)
    tsb = pool.bf16(TC)
    bc32_t = pool.f32(TC)
    bc32 = bc32_t[0:32, :]
    wdt_b = pool.bf16(D)
    sp0 = pool.f32(TC)
    sp1 = pool.f32(TC)
    sp2 = pool.f32(TC)

    for g in range(16):
        if ch == 0:
            p.dma("sync", xf[:, 0:3], hd["halo3"][g * 128:(g + 1) * 128, :])
        else:
            p.dma("sync", xf[:, 0:3],
                  x2sc[g * 128:(g + 1) * 128, ch * TC - 3:ch * TC])
        p.dma("sync", xf[:, 3:TCH],
              x2sc[g * 128:(g + 1) * 128, ch * TC:(ch + 1) * TC])
        p.vcopy(xb[:, g * TCH:(g + 1) * TCH], xf[:, 0:TCH])
    _load_vec(p, cb_s, hd["cb4"][:, :])
    _load_vec(p, dtb_s, hd["dtb"][:, :])
    p.dma("sync", cw_s.rearrange("p (k o) -> p k o", k=32),
          hd["cw4"][:, :].rearrange("(k p) o -> p k o", p=128))
    _load_wslab(p, wscr, wdt_b, hd["wdtT"][:, :])

    cs = slice(ch * TC, (ch + 1) * TC)
    for m in range(32):
        _load_wslab(p, wscr, wslab, hd["wipT"][:, m * 128:(m + 1) * 128])
        for k in range(16):
            p.mm(ps["a"][:, :], wslab[:, k * 128:(k + 1) * 128],
                 xb[:, k * TCH:k * TCH + TC], start=(k == 0), stop=(k == 15))
        for k in range(16):
            p.mm(ps["s1"][:, 0:3], wslab[:, k * 128:(k + 1) * 128],
                 xb[:, k * TCH + TC:(k + 1) * TCH],
                 start=(k == 0), stop=(k == 15))
        p.copy(win[:, 0:TC], ps["a"][:, :])
        p.copy(win[:, TC:TCH], ps["s1"][:, 0:3])
        p.ts(ut[:, :], win[:, 0:TC], cw_s[:, 4 * m:4 * m + 1])
        for i in range(1, 4):
            p.stt(ut[:, :], win[:, i:i + TC],
                  cw_s[:, 4 * m + i:4 * m + i + 1], ut[:, :],
                  ALU.mult, ALU.add)
        p.ts(ut[:, :], ut[:, :], cb_s[:, m:m + 1], op0=ALU.add)
        p.act(ut[:, :], ut[:, :], AF.Silu)
        p.dma("gpsimd", hd["ua_in"][m * 128:(m + 1) * 128, cs], ut[:, :])
        p.copy(ub[:, m * TC:(m + 1) * TC], ut[:, :])
    for m in range(32):
        _load_wslab(p, wscr, wslab,
                    hd["wipT"][:, D + m * 128:D + (m + 1) * 128])
        for k in range(16):
            p.mm(ps["a"][:, :], wslab[:, k * 128:(k + 1) * 128],
                 xb[:, k * TCH + 3:(k + 1) * TCH],
                 start=(k == 0), stop=(k == 15))
        p.act(ut[:, :], ps["a"][:, :], AF.Silu)
        p.dma("gpsimd", hd["sga_in"][m * 128:(m + 1) * 128, cs], ut[:, :])
    for k in range(32):
        _load_wslab(p, wscr, wslab, hd["wxpT"][k * 128:(k + 1) * 128, :])
        p.mm(ps["a"][:, :], wslab[:, 0:128], ub[:, k * TC:(k + 1) * TC],
             start=(k == 0), stop=(k == 31), sync=True)
        p.mm(ps["s2"][0:32, :], wslab[:, 128:160], ub[:, k * TC:(k + 1) * TC],
             start=(k == 0), stop=(k == 31), sync=True)
    p.copy(sp0[:, :], ps["a"][:, :])
    p.vcopy(bc32[:, :], ps["s2"][0:32, :])
    p.dma("gpsimd", hd["bcga_in"][0:32, cs], bc32[:, :])
    p.copy(tsb[:, :], sp0[:, :])
    for m in range(32):
        p.mm(ps["a"][:, :], wdt_b[:, m * 128:(m + 1) * 128], tsb[:, :],
             start=True, stop=True)
        p.ts(sp0[:, :], ps["a"][:, :], dtb_s[:, m:m + 1], op0=ALU.add)
        _softplus(p, sp0[:, :], sp0[:, :], sp1[:, :], sp2[:, :])
        p.dma("gpsimd", hd["dta_in"][m * 128:(m + 1) * 128, cs], sp0[:, :])


def _emit_D(p, nc, pool, ps, hd):
    DC, N = 512, 16
    TD = 1024
    ND = S // TD
    sels = hd["_sel16"]
    A_s = pool.f32(4 * N)
    Dp_s = pool.f32(4)
    uf = pool.f32(4 * TD)
    dtf = pool.f32(4 * TD)
    du = pool.f32(4 * TD)
    yt = pool.f32(4 * TD)
    bm16_t = pool.f32(TD)
    bm16 = bm16_t[0:16, :]
    cm16_t = pool.f32(TD)
    cm16 = cm16_t[0:16, :]
    bbc = pool.f32(TD)
    cbc = pool.f32(TD)
    da = pool.f32(TD)
    dbu = pool.f32(TD)
    hsc = pool.f32(TD)
    tmp = pool.f32(TD)
    carry = pool.f32(64)

    p.dma("sync", A_s.rearrange("p (k n) -> p k n", k=4),
          hd["Alog"][:, :].rearrange("(k p) n -> p k n", p=128))
    p.act(A_s, A_s, AF.Exp)
    p.ts(A_s, A_s, -1.0)
    _load_vec(p, Dp_s, hd["Dp"][:, :])

    for it in range(ND):
        t0 = it * TD
        s2 = t0 // 2048
        off = t0 % 2048
        for d in range(4):
            p.dma("sync", uf[:, d * TD:(d + 1) * TD],
                  hd["ua_out"][s2 * 512 + d * 128:s2 * 512 + (d + 1) * 128,
                               off:off + TD])
            p.dma("sync", dtf[:, d * TD:(d + 1) * TD],
                  hd["dta_out"][s2 * 512 + d * 128:s2 * 512 + (d + 1) * 128,
                                off:off + TD])
            p.mul(du[:, d * TD:(d + 1) * TD], dtf[:, d * TD:(d + 1) * TD],
                  uf[:, d * TD:(d + 1) * TD])
        p.dma("sync", bm16[:, :],
              hd["bcga_out"][s2 * 32:s2 * 32 + 16, off:off + TD])
        p.dma("sync", cm16[:, :],
              hd["bcga_out"][s2 * 32 + 16:s2 * 32 + 32, off:off + TD])
        for n in range(N):
            for half in range(2):
                sl = slice(half * TC, (half + 1) * TC)
                p.mm(ps["a"][:, :], sels[:, n * 128:(n + 1) * 128],
                     bm16[:, sl], start=True, stop=True)
                p.copy(bbc[:, sl], ps["a"][:, :])
                p.mm(ps["a"][:, :], sels[:, n * 128:(n + 1) * 128],
                     cm16[:, sl], start=True, stop=True)
                p.copy(cbc[:, sl], ps["a"][:, :])
            for d in range(4):
                dsl = slice(d * TD, (d + 1) * TD)
                ci = n * 4 + d
                p.act(da[:, :], dtf[:, dsl], AF.Exp,
                      scale=A_s[:, d * N + n:d * N + n + 1])
                p.mul(dbu[:, :], du[:, dsl], bbc[:, :])
                if it == 0:
                    p.scan(hsc[:, :], da[:, :], dbu[:, :], 0.0)
                else:
                    p.scan(hsc[:, :], da[:, :], dbu[:, :],
                           carry[:, ci:ci + 1])
                p.vcopy(carry[:, ci:ci + 1], hsc[:, TD - 1:TD])
                p.mul(tmp[:, :], hsc[:, :], cbc[:, :])
                if n == 0:
                    p.vcopy(yt[:, dsl], tmp[:, :])
                else:
                    p.add(yt[:, dsl], yt[:, dsl], tmp[:, :])
        for d in range(4):
            dsl = slice(d * TD, (d + 1) * TD)
            p.stt(yt[:, dsl], uf[:, dsl], Dp_s[:, d:d + 1], yt[:, dsl],
                  ALU.mult, ALU.add)
            p.dma("sync", da[:, :],
                  hd["sga_out"][s2 * 512 + d * 128:s2 * 512 + (d + 1) * 128,
                                off:off + TD])
            p.mul(yt[:, dsl], yt[:, dsl], da[:, :])
            p.dma("gpsimd",
                  hd["ya_in"][s2 * 512 + d * 128:s2 * 512 + (d + 1) * 128,
                              off:off + TD], yt[:, dsl])


def _emit_E_chunk(p, nc, pool, ps, hd, ch):
    D, INTER = 4096, 5632
    NI = INTER // 128
    ones_col, ones_row = hd["_ones_col"], hd["_ones_row"]
    cs = slice(ch * TC, (ch + 1) * TC)
    scr = pool.f32(TC)
    rt = pool.f32(16 * TC)
    gt = pool.f32(TC)
    ut2 = pool.f32(TC)
    wscr = pool.f32(2048)
    gs_s = pool.f32(NI)
    us_s = pool.f32(NI)
    ds_s = pool.f32(16)
    sq_scr = pool.f32(TC)
    rstd_bc = pool.f32(TC)
    mr_bc = pool.f32(TC)
    lnr_t = pool.f32(5 * TC)
    lnr = lnr_t[0:1, :]
    x3b = pool.bf16(16 * TC)
    un_e = pool.bf16(NI * TC)
    yb = un_e[:, 0:32 * TC]
    hdn = un_e
    wslab = pool.bf16(4096)
    w3slab = pool.bf16(NI * 128)

    _load_vec(p, gs_s, hd["gs"][:, :])
    _load_vec(p, us_s, hd["us"][:, :])
    _load_vec(p, ds_s, hd["ds"][:, :])
    for g in range(32):
        p.dma("sync", scr[:, :], hd["ya_out"][g * 128:(g + 1) * 128, cs])
        p.vcopy(yb[:, g * TC:(g + 1) * TC], scr[:, :])
    for m in range(16):
        _load_wslab(p, wscr, wslab, hd["wopT"][:, m * 128:(m + 1) * 128])
        for k in range(32):
            p.mm(ps["a"][:, :], wslab[:, k * 128:(k + 1) * 128],
                 yb[:, k * TC:(k + 1) * TC], start=(k == 0), stop=(k == 31))
        p.dma("sync", scr[:, :], hd["x2sc"][m * 128:(m + 1) * 128, cs])
        p.add(rt[:, m * TC:(m + 1) * TC], ps["a"][:, :], scr[:, :])
    r_tiles = [rt[:, g * TC:(g + 1) * TC] for g in range(16)]
    _ln_ct(p, r_tiles, r_tiles, ones_col, ones_row, ps["s1"], ps["s2"],
           ps["bc"], sq_scr, lnr, rstd_bc, mr_bc, H, TC)
    for g in range(16):
        p.copy(x3b[:, g * TC:(g + 1) * TC], r_tiles[g])
        p.dma("gpsimd", hd["x3s"][g * 128:(g + 1) * 128, cs], r_tiles[g])
    for ig in range(NI):
        _load_wslab(p, wscr, wslab, hd["gw1T"][:, ig * 128:(ig + 1) * 128])
        for k in range(16):
            p.mm(ps["a"][:, :], wslab[:, k * 128:(k + 1) * 128],
                 x3b[:, k * TC:(k + 1) * TC], start=(k == 0), stop=(k == 15))
        p.ts(gt[:, :], ps["a"][:, :], gs_s[:, ig:ig + 1])
        p.act(gt[:, :], gt[:, :], AF.Silu)
        _load_wslab(p, wscr, wslab, hd["uwT"][:, ig * 128:(ig + 1) * 128])
        for k in range(16):
            p.mm(ps["a"][:, :], wslab[:, k * 128:(k + 1) * 128],
                 x3b[:, k * TC:(k + 1) * TC], start=(k == 0), stop=(k == 15))
        p.ts(ut2[:, :], ps["a"][:, :], us_s[:, ig:ig + 1])
        p.mul(hdn[:, ig * TC:(ig + 1) * TC], gt[:, :], ut2[:, :])
    for m in range(16):
        _load_wslab(p, wscr, w3slab, hd["dwT"][:, m * 128:(m + 1) * 128])
        for ig in range(NI):
            p.mm(ps["a"][:, :], w3slab[:, ig * 128:(ig + 1) * 128],
                 hdn[:, ig * TC:(ig + 1) * TC],
                 start=(ig == 0), stop=(ig == NI - 1))
        p.dma("sync", scr[:, :], hd["x3s"][m * 128:(m + 1) * 128, cs])
        p.stt(rt[:, m * TC:(m + 1) * TC], ps["a"][:, :],
              ds_s[:, m:m + 1], scr[:, :], ALU.mult, ALU.add)
    _ln_ct(p, r_tiles, r_tiles, ones_col, ones_row, ps["s1"], ps["s2"],
           ps["bc"], sq_scr, lnr, rstd_bc, mr_bc, H, TC)
    for g in range(16):
        p.dma("gpsimd", hd["outc"][g * 128:(g + 1) * 128, cs], r_tiles[g])


def build_fused():
    nc = bass.Bass("TRN2", target_bir_lowering=False)
    hd = {}
    hd["xT"] = nc.dram_tensor("xT", [H, S], F32, kind="ExternalInput")
    hd["wxyT"] = nc.dram_tensor("wxyT", [H, 512], F32, kind="ExternalInput")
    for nm, shp in (("igw", [256, 128]), ("agw", [256, 128]),
                    ("igb", [256, 1]), ("agb", [256, 1]), ("apar", [256, 1]),
                    ("cw", [256, 4]), ("cb", [256, 1])):
        hd[nm] = nc.dram_tensor(nm, shp, F32, kind="ExternalInput")
    hd["xcore"] = nc.dram_tensor("xcore", [H, 2048], F32,
                                 kind="ExternalInput")
    for nm, shp in (("wresT", [H, H]), ("gwT", [H, 8]),
                    ("w1T", [8, H, 1408]), ("w2T", [8, H, 1408]),
                    ("w3T", [8, 1408, H]), ("sw1T", [H, 2816]),
                    ("sw2T", [H, 2816]), ("sw3T", [2816, H]),
                    ("eye", [128, 128]), ("sel8", [8, 1024]),
                    ("gmask", [128, 8]),
                    ("wipT", [H, 8192]), ("wxpT", [4096, 160]),
                    ("wdtT", [128, 4096]), ("dtb", [4096, 1]),
                    ("cw4", [4096, 4]), ("cb4", [4096, 1]),
                    ("Alog", [512, 16]), ("Dp", [512, 1]),
                    ("sel16", [16, 2048]),
                    ("wopT", [4096, H]), ("gw1T", [H, 5632]),
                    ("uwT", [H, 5632]), ("dwT", [5632, H]),
                    ("gs", [5632, 1]), ("us", [5632, 1]), ("ds", [H, 1])):
        hd[nm] = nc.dram_tensor(nm, shp, F32, kind="ExternalInput")
    for nm, shp in (("za_in", [H, 2048]), ("za_out", [H, 2048]),
                    ("x1s", [H, 2048]), ("x2sc", [H, 2048]),
                    ("tail_in", [H, 3]), ("tail_out", [NCORE * H, 3]),
                    ("halo3", [H, 3]),
                    ("bcga_in", [32, 2048]), ("bcga_out", [256, 2048]),
                    ("ua_in", [4096, 2048]), ("ua_out", [4096, 2048]),
                    ("dta_in", [4096, 2048]), ("dta_out", [4096, 2048]),
                    ("sga_in", [4096, 2048]), ("sga_out", [4096, 2048]),
                    ("ya_in", [4096, 2048]), ("ya_out", [4096, 2048]),
                    ("x3s", [H, 2048])):
        hd[nm] = nc.dram_tensor(nm, shp, F32)
    hd["outc"] = nc.dram_tensor("outc", [H, 2048], F32, kind="ExternalOutput")
    if FDEBUG:
        for nm, shp in (("dbg_za", [H, 2048]), ("dbg_x2", [H, 2048]),
                        ("dbg_ua", [4096, 2048]), ("dbg_dta", [4096, 2048]),
                        ("dbg_bc", [256, 2048]), ("dbg_ya", [4096, 2048]),
                        ("dbg_halo", [H, 3])):
            hd[nm] = nc.dram_tensor(nm, shp, F32, kind="ExternalOutput")

    with contextlib.ExitStack() as st:
        e = st.enter_context
        pool = Pool(nc, st, "pool", 43008)
        eyes = e(nc.sbuf_tensor("eyes", [128, 128], F32))
        sels8 = e(nc.sbuf_tensor("sels8", [8, 1024], F32))
        sel16 = e(nc.sbuf_tensor("sel16s", [16, 2048], F32))
        gmask = e(nc.sbuf_tensor("gmasks", [128, 8], F32))
        ones_col = e(nc.sbuf_tensor("ones_col", [128, 1], F32))
        ones_row = e(nc.sbuf_tensor("ones_row", [1, 128], F32))
        ps = {
            "a": e(nc.psum_tensor("ps_a", [128, TC], F32)),
            "s1": e(nc.psum_tensor("ps_s1", [128, TC], F32)),
            "s2": e(nc.psum_tensor("ps_s2", [128, TC], F32)),
            "bc": e(nc.psum_tensor("ps_bc", [128, TC], F32)),
        }
        sem = e(nc.semaphore("G"))
        ccsems = [e(nc.semaphore(f"CC{i}")) for i in range(8)]
        block = e(nc.Block())
        hd["_eyes"] = eyes
        hd["_sels8"] = sels8
        hd["_sel16"] = sel16
        hd["_gmask"] = gmask
        hd["_ones_col"] = ones_col
        hd["_ones_row"] = ones_row

        p = Prog(nc, sem)
        p.memset(ones_col[:, :], 1.0)
        p.memset(ones_row[:, :], 1.0)
        p.dma("sync", eyes[:, :], hd["eye"][:, :])
        p.dma("sync", sels8[:, :], hd["sel8"][:, :])
        p.dma("sync", sel16[:, :], hd["sel16"][:, :])
        p.dma("sync", gmask[:, :], hd["gmask"][:, :])

        _emit_A(p, nc, pool, ps, hd)
        _collective(p, "AllToAll", [hd["za_in"][:, :]], [hd["za_out"][:, :]],
                    ccsems[0])
        for ch in range(4):
            pool.reset()
            _emit_B_chunk(p, nc, pool, ps, hd, ch)
        p.dma("gpsimd", hd["tail_in"][:, :], hd["x2sc"][:, 2045:2048])
        _collective(p, "AllGather", [hd["tail_in"][:, :]],
                    [hd["tail_out"][:, :]], ccsems[1])
        pool.reset()
        _emit_halo(p, nc, pool, hd)
        for ch in range(4):
            pool.reset()
            _emit_C_chunk(p, nc, pool, ps, hd, ch)
        _collective(p, "AllToAll", [hd["ua_in"][:, :]], [hd["ua_out"][:, :]],
                    ccsems[2])
        _collective(p, "AllToAll", [hd["dta_in"][:, :]],
                    [hd["dta_out"][:, :]], ccsems[3])
        _collective(p, "AllToAll", [hd["sga_in"][:, :]],
                    [hd["sga_out"][:, :]], ccsems[4])
        _collective(p, "AllGather", [hd["bcga_in"][:, :]],
                    [hd["bcga_out"][:, :]], ccsems[5])
        pool.reset()
        _emit_D(p, nc, pool, ps, hd)
        _collective(p, "AllToAll", [hd["ya_in"][:, :]], [hd["ya_out"][:, :]],
                    ccsems[6])
        for ch in range(4):
            pool.reset()
            _emit_E_chunk(p, nc, pool, ps, hd, ch)
        if FDEBUG:
            for src_nm, dst_nm in (("za_out", "dbg_za"), ("x2sc", "dbg_x2"),
                                   ("ua_out", "dbg_ua"), ("dta_out", "dbg_dta"),
                                   ("bcga_out", "dbg_bc"), ("ya_out", "dbg_ya"),
                                   ("halo3", "dbg_halo")):
                p.dma("gpsimd", hd[dst_nm][:, :], hd[src_nm][:, :])
        p.build(block)
    return nc


def kernel_fused(inp):
    f32 = np.float32
    x = np.asarray(inp["L0_attn_x"], f32).reshape(S, H)
    xT = _ct(x.T)
    W_xy = np.asarray(inp["L0_attn_W_xy"], f32)
    ig_w = np.asarray(inp["L0_attn_ig_w"], f32)
    ig_b = np.asarray(inp["L0_attn_ig_b"], f32)
    ag_w = np.asarray(inp["L0_attn_ag_w"], f32)
    ag_b = np.asarray(inp["L0_attn_ag_b"], f32)
    a_par = np.asarray(inp["L0_attn_a_param"], f32)
    conv_w = np.asarray(inp["L0_attn_conv_w"], f32)
    conv_b = np.asarray(inp["L0_attn_conv_b"], f32)
    wresT = _ct(np.asarray(inp["L0_attn_W_resid"], f32).T)
    gwT0 = _ct(np.asarray(inp["L0_ffn_gate_w"], f32).T)
    w1T = _ct(np.asarray(inp["L0_ffn_w1"], f32).transpose(0, 2, 1))
    w2T = _ct(np.asarray(inp["L0_ffn_w2"], f32).transpose(0, 2, 1))
    w3T = _ct(np.asarray(inp["L0_ffn_w3"], f32).transpose(0, 2, 1))
    sw1T = _ct(np.asarray(inp["L0_ffn_sw1"], f32).T)
    sw2T = _ct(np.asarray(inp["L0_ffn_sw2"], f32).T)
    sw3T = _ct(np.asarray(inp["L0_ffn_sw3"], f32).T)
    eye = np.eye(128, dtype=f32)
    sel8 = _eye_sel(8)
    sel16 = _eye_sel(16)
    wipT = _ct(np.asarray(inp["L1_attn_in_proj_w"], f32).T)
    wxpT = _ct(np.asarray(inp["L1_attn_x_proj_w"], f32).T)
    wdtT = _ct(np.asarray(inp["L1_attn_dt_proj_w"], f32).T)
    dtb = _ct(np.asarray(inp["L1_attn_dt_proj_b"], f32).reshape(4096, 1))
    cw4 = _ct(np.asarray(inp["L1_attn_conv_w"], f32).reshape(4096, 4))
    cb4 = _ct(np.asarray(inp["L1_attn_conv_b"], f32).reshape(4096, 1))
    Alog = np.asarray(inp["L1_attn_A_log"], f32)
    Dp = np.asarray(inp["L1_attn_D_param"], f32).reshape(4096, 1)
    wopT = _ct(np.asarray(inp["L1_attn_out_proj_w"], f32).T)
    gw1T = _ct(np.asarray(inp["L1_ffn_gate_w"], f32).T)
    uwT = _ct(np.asarray(inp["L1_ffn_up_w"], f32).T)
    dwT = _ct(np.asarray(inp["L1_ffn_down_w"], f32).T)
    gs = _ct(np.asarray(inp["L1_ffn_gate_s"], f32).reshape(-1, 1))
    us = _ct(np.asarray(inp["L1_ffn_up_s"], f32).reshape(-1, 1))
    ds = _ct(np.asarray(inp["L1_ffn_down_s"], f32).reshape(-1, 1))

    ncF = _prog("F", build_fused)
    in_maps = []
    for c in range(NCORE):
        r0 = c * 256
        d0 = c * 512
        wsl = np.concatenate([W_xy[r0:r0 + 256], W_xy[H + r0:H + r0 + 256]],
                             axis=0)
        gmask = np.zeros((128, 8), f32)
        if c > 0:
            gmask[:, c - 1] = 1.0
        in_maps.append({
            "xT": xT, "wxyT": _ct(wsl.T),
            "igw": _ct(ig_w[2 * c:2 * c + 2].reshape(256, 128)),
            "igb": _ct(ig_b[2 * c:2 * c + 2].reshape(256, 1)),
            "agw": _ct(ag_w[2 * c:2 * c + 2].reshape(256, 128)),
            "agb": _ct(ag_b[2 * c:2 * c + 2].reshape(256, 1)),
            "apar": _ct(a_par[r0:r0 + 256].reshape(256, 1)),
            "cw": _ct(conv_w[:, r0:r0 + 256].T),
            "cb": _ct(conv_b[r0:r0 + 256].reshape(256, 1)),
            "xcore": _ct(xT[:, c * 2048:(c + 1) * 2048]),
            "wresT": wresT, "gwT": gwT0, "w1T": w1T, "w2T": w2T, "w3T": w3T,
            "sw1T": sw1T, "sw2T": sw2T, "sw3T": sw3T,
            "eye": eye, "sel8": sel8, "gmask": gmask,
            "wipT": wipT, "wxpT": wxpT, "wdtT": wdtT, "dtb": dtb,
            "cw4": cw4, "cb4": cb4,
            "Alog": _ct(Alog[d0:d0 + 512]), "Dp": _ct(Dp[d0:d0 + 512]),
            "sel16": sel16,
            "wopT": wopT, "gw1T": gw1T, "uwT": uwT, "dwT": dwT,
            "gs": gs, "us": us, "ds": ds,
        })
    res = run_bass_kernel_spmd(ncF, in_maps, core_ids=list(range(NCORE)))
    outT = np.concatenate([res.results[c]["outc"] for c in range(NCORE)],
                          axis=1)
    return _ct(outT.T).reshape(1, S, H).astype(np.float32)
